# version 27
# baseline (speedup 1.0000x reference)
import numpy as np
import ml_dtypes
from contextlib import ExitStack

import concourse.bass as bass
import concourse.mybir as mybir
from concourse.bass_utils import run_bass_kernel_spmd

F32 = mybir.dt.float32
BF16 = mybir.dt.bfloat16
I32 = mybir.dt.int32
AF = mybir.ActivationFunctionType
ALU = mybir.AluOpType

P = 128
D = 1024
DC = D // P
TSEQ = 4096
TT = 512
NT = TSEQ // TT
H = 3072
HC = H // P
SF = 1.0 / 127.0
EPS = 1e-5
DINV = 1.0 / D

ENGINES = ("tensor", "vector", "scalar", "syq", "gpq")


class Buf:
    __slots__ = ("name", "writer", "readers")

    def __init__(self, name):
        self.name = name
        self.writer = None
        self.readers = []


NLANES = 8


class Sched:
    def __init__(self):
        self.per = {e: [] for e in ENGINES}
        self.qcount = {e: 0 for e in ENGINES}
        self.nops = 0
        self.barrier = []
        self.last_by_sem = {}

    def add(self, eng, emit, reads=(), writes=(), dma=False, force_inc=False):
        op = {"eng": eng, "emit": emit, "dma": dma, "deps": None,
              "inc": force_inc or dma, "val": None, "sem": eng}
        if dma:
            k = self.qcount[eng]
            self.qcount[eng] = k + 1
            op["sem"] = f"{eng}{k % NLANES}"
            op["val"] = 16 * (k // NLANES + 1)
        self.nops += 1
        depmap = {}
        for b in self.barrier:
            depmap[id(b)] = b
        for b in reads:
            if b.writer is not None:
                depmap[id(b.writer)] = b.writer
            b.readers.append(op)
        for b in writes:
            if b.writer is not None:
                depmap[id(b.writer)] = b.writer
            for r in b.readers:
                depmap[id(r)] = r
            b.writer = op
            b.readers = []
        op["deps"] = [d for d in depmap.values() if d is not op]
        self.per[eng].append(op)
        self.last_by_sem[op["sem"]] = op
        return op

    def phase_barrier(self):
        self.barrier = list(self.last_by_sem.values())

    def finalize(self):
        for e in ENGINES:
            for op in self.per[e]:
                for d in op["deps"]:
                    if d["eng"] != op["eng"]:
                        d["inc"] = True
        totals = {}
        for e in ENGINES:
            cnt = 0
            for op in self.per[e]:
                if op["inc"] and not op["dma"]:
                    cnt += 1
                    op["val"] = cnt
            totals[e] = cnt
        for e in ENGINES:
            n = self.qcount[e]
            for lane in range(min(n, NLANES)):
                cnt_lane = (n - lane + NLANES - 1) // NLANES
                totals[f"{e}{lane}"] = 16 * cnt_lane
        return totals

    def emit_stream(self, eng, eng_obj, sems):
        watermark = {}
        for op in self.per[eng]:
            waits = {}
            for d in op["deps"]:
                if d["eng"] == eng or d["val"] is None:
                    continue
                if d["val"] > waits.get(d["sem"], 0):
                    waits[d["sem"]] = d["val"]
            for de in sorted(waits):
                v = waits[de]
                if v > watermark.get(de, 0):
                    eng_obj.wait_ge(sems[de], v)
                    watermark[de] = v
            instr = op["emit"](eng_obj)
            if op["inc"]:
                instr.then_inc(sems[op["sem"]], 16 if op["dma"] else 1)


def build_nc(debug=False):
    nc = bass.Bass("TRN2", target_bir_lowering=False,
                   detect_race_conditions=False)
    sch = Sched()
    scratch_kind = "ExternalOutput" if debug else "Internal"

    dx = nc.dram_tensor("xt", [P, DC, TSEQ], F32, kind="ExternalInput").ap()
    dwk = nc.dram_tensor("wkt", [P, DC, D], F32, kind="ExternalInput").ap()
    dwv = nc.dram_tensor("wvt", [P, DC, D], F32, kind="ExternalInput").ap()
    dwr = nc.dram_tensor("wrt", [P, DC, D], F32, kind="ExternalInput").ap()
    dwo = nc.dram_tensor("wot", [P, DC, D], F32, kind="ExternalInput").ap()
    dwg = nc.dram_tensor("wgt", [P, DC, H], I32, kind="ExternalInput").ap()
    dwu = nc.dram_tensor("wut", [P, DC, H], I32, kind="ExternalInput").ap()
    dwd = nc.dram_tensor("wdt", [P, HC, D], I32, kind="ExternalInput").ap()
    dtd = nc.dram_tensor("td", [P, DC], F32, kind="ExternalInput").ap()
    dtf = nc.dram_tensor("tf", [P, DC], F32, kind="ExternalInput").ap()
    dmk = nc.dram_tensor("mk", [P, DC], F32, kind="ExternalInput").ap()
    dmv = nc.dram_tensor("mv", [P, DC], F32, kind="ExternalInput").ap()
    dmr = nc.dram_tensor("mr", [P, DC], F32, kind="ExternalInput").ap()
    dout = nc.dram_tensor("out", [P, DC, TSEQ], F32, kind="ExternalOutput").ap()
    d_e = nc.dram_tensor("e_h", [P, DC, TSEQ], F32, kind=scratch_kind).ap()
    d_ev = nc.dram_tensor("ev_h", [P, DC, TSEQ], F32, kind=scratch_kind).ap()
    d_r = nc.dram_tensor("r_h", [P, DC, TSEQ], F32, kind=scratch_kind).ap()
    d_xp = nc.dram_tensor("xp_h", [P, DC, TSEQ], F32, kind=scratch_kind).ap()
    d_wgbf = nc.dram_tensor("wg_bf", [P, DC, H], BF16, kind="Internal").ap()
    d_wubf = nc.dram_tensor("wu_bf", [P, DC, H], BF16, kind="Internal").ap()
    d_wdbf = nc.dram_tensor("wd_bf", [P, HC, D], BF16, kind="Internal").ap()
    B_wgbf = Buf("d_wgbf"); B_wubf = Buf("d_wubf"); B_wdbf = Buf("d_wdbf")
    B_de = [Buf(f"d_e{i}") for i in range(NT)]
    B_dev = [Buf(f"d_ev{i}") for i in range(NT)]
    B_dr = [Buf(f"d_r{i}") for i in range(NT)]
    B_dxp = [Buf(f"d_xp{i}") for i in range(NT)]

    es_all = ExitStack()
    sems = {e: es_all.enter_context(nc.semaphore(f"sem_{e}")) for e in ENGINES}
    for e in ("syq", "gpq"):
        for lane in range(NLANES):
            sems[f"{e}{lane}"] = es_all.enter_context(
                nc.semaphore(f"sem_{e}{lane}"))

    pb = [es_all.enter_context(nc.psum_tensor(f"pb{i}", [P, TT], F32))
          for i in range(8)]
    pbB = [Buf(f"pb{i}") for i in range(8)]

    def sb(es, name, shape, dtype):
        h = es.enter_context(nc.sbuf_tensor(name, shape, dtype))
        return h, Buf(name)

    pers = ExitStack()
    ones_bf, B_ones_bf = sb(pers, "s_ones_bf", [P, P], BF16)
    ones_f, B_ones_f = sb(pers, "s_ones_f", [P, P], F32)
    lam_sb, B_lam = sb(pers, "s_lam", [P, DC], F32)
    eu_sb, B_eu = sb(pers, "s_eu", [P, DC], F32)
    mk_sb, B_mk = sb(pers, "s_mk", [P, DC], F32)
    mv_sb, B_mv = sb(pers, "s_mv", [P, DC], F32)
    mr_sb, B_mr = sb(pers, "s_mr", [P, DC], F32)
    vtmp, B_vtmp = sb(pers, "s_vtmp", [P, DC], F32)
    cst, B_cst = sb(pers, "s_cst", [P, 3], F32)
    CONST_COL = {EPS: 0, 1.0: 1, 30.0: 2}

    def ld(dst_ap, src_ap, reads, writes, inc=False):
        return sch.add("syq", lambda e, o=dst_ap, i=src_ap: e.dma_start(o, i),
                       reads=reads, writes=writes, dma=True, force_inc=inc)

    def ldg(dst_ap, src_ap, reads, writes, inc=False):
        return sch.add("gpq", lambda e, o=dst_ap, i=src_ap: e.dma_start(o, i),
                       reads=reads, writes=writes, dma=True, force_inc=inc)

    def st(dst_ap, src_ap, reads, writes=(), inc=False):
        return sch.add("gpq", lambda e, o=dst_ap, i=src_ap: e.dma_start(o, i),
                       reads=reads, writes=writes, dma=True, force_inc=inc)

    def top(emit, reads, writes):
        return sch.add("tensor", emit, reads, writes)

    def vop(emit, reads, writes):
        return sch.add("vector", emit, reads, writes)

    def sop(emit, reads, writes):
        return sch.add("scalar", emit, reads, writes)

    def mm(pbank_i, lhsT, rhs, start, stop, reads):
        return top(lambda e, o=pb[pbank_i][:], l=lhsT, r=rhs, a=start, b=stop:
                   e.matmul(o, l, r, start=a, stop=b),
                   reads=reads, writes=(pbB[pbank_i],))

    def act(out_ap, in_ap, func, reads, writes, scale=1.0, bias=0.0):
        if isinstance(bias, float) and bias != 0.0:
            col = CONST_COL[bias]
            bias = cst[:, col:col + 1]
            reads = tuple(reads) + (B_cst,)
        return sop(lambda e, o=out_ap, i=in_ap, f=func, s=scale, b=bias:
                   e.activation(o, i, f, bias=b, scale=s),
                   reads=reads, writes=writes)

    def tt(out_ap, a_ap, b_ap, op, reads, writes):
        return vop(lambda e, o=out_ap, x=a_ap, y=b_ap, p=op:
                   e.tensor_tensor(o, x, y, p), reads, writes)

    def stt(out_ap, in0_ap, scal_ap, in1_ap, op0, op1, reads, writes):
        return vop(lambda e, o=out_ap, i0=in0_ap, s=scal_ap, i1=in1_ap,
                   p0=op0, p1=op1: e.scalar_tensor_tensor(o, i0, s, i1, p0, p1),
                   reads, writes)

    def tbl(e):
        try:
            from concourse.hw_specs import get_activation_tables
            idx = list(get_activation_tables(e.bass.m.arch).keys()
                       ).index("natural_log_exp_and_others")
        except Exception:
            idx = 6
        return e.add_instruction(mybir.InstLoadActFuncSet(
            name=e.bass.get_next_instruction_name(),
            act_func_set_id=idx, ins=[], outs=[]))
    sch.add("scalar", tbl, (), ())

    for val, col in CONST_COL.items():
        vop(lambda e, c=col, v=val: e.memset(cst[:, c:c + 1], v),
            (), (B_cst,))
    vop(lambda e: e.memset(ones_bf[:], 1.0), (), (B_ones_bf,))
    vop(lambda e: e.memset(ones_f[:], 1.0), (), (B_ones_f,))
    ld(mk_sb[:], dmk, (), (B_mk,))
    ld(mv_sb[:], dmv, (), (B_mv,))
    ld(mr_sb[:], dmr, (), (B_mr,))
    ld(vtmp[:], dtd, (), (B_vtmp,))
    act(lam_sb[:], vtmp[:], AF.Exp, (B_vtmp,), (B_lam,))
    act(lam_sb[:], lam_sb[:], AF.Exp, (B_lam,), (B_lam,), scale=-1.0)
    ld(vtmp[:], dtf, (B_lam,), (B_vtmp,))
    act(eu_sb[:], vtmp[:], AF.Exp, (B_vtmp,), (B_eu,))

    def ln_stats(pbS, pbQ, m_h, m_B, sA_h, sA_B, sB_h, sB_B):
        act(m_h[:], pb[pbS][:], AF.Identity, (pbB[pbS],), (m_B,), scale=DINV)
        act(sA_h[:], m_h[:], AF.Square, (m_B,), (sA_B,))
        stt(sB_h[:], pb[pbQ][:], DINV, sA_h[:], ALU.mult, ALU.subtract,
            (pbB[pbQ], sA_B), (sB_B,))
        act(sA_h[:], sB_h[:], AF.Ln, (sB_B,), (sA_B,), bias=EPS)
        act(sB_h[:], sA_h[:], AF.Exp, (sA_B,), (sB_B,), scale=-0.5)

    def bc(h, n_mid):
        return h[:].unsqueeze(1).broadcast_to([P, n_mid, TT])

    p1 = ExitStack()
    wk_sb, B_wk = sb(p1, "p1_wk", [P, DC, D], BF16)
    wv_sb, B_wv = sb(p1, "p1_wv", [P, DC, D], BF16)
    wr_sb, B_wr = sb(p1, "p1_wr", [P, DC, D], BF16)
    x_sb, B_x1 = sb(p1, "p1_x", [P, DC, TT], F32)
    xbf, B_xbf = sb(p1, "p1_xbf", [P, DC, TT], BF16)
    x2bf, B_x2bf = sb(p1, "p1_x2bf", [P, DC, TT], BF16)
    xhat, B_xhat = sb(p1, "p1_xhat", [P, DC, TT + 1], BF16)
    tmpa, B_tmpa = sb(p1, "p1_tmpa", [P, DC, TT], BF16)
    xqbf = {}
    B_xqbf = {}
    for q in ("k", "v", "r"):
        for b in range(2):
            h, bb = sb(p1, f"p1_x{q}bf{b}", [P, DC, TT], BF16)
            xqbf[(q, b)] = h; B_xqbf[(q, b)] = bb
    E_sb, B_E = sb(p1, "p1_E", [P, DC, TT], F32)
    v_sb, B_v = sb(p1, "p1_v", [P, DC, TT], F32)
    ev_sb, B_ev = sb(p1, "p1_ev", [P, DC, TT], F32)
    m1, B_m1 = sb(p1, "p1_m", [P, TT], F32)
    sA1, B_sA1 = sb(p1, "p1_sA", [P, TT], F32)
    sB1, B_sB1 = sb(p1, "p1_sB", [P, TT], F32)

    ldg(wk_sb[:], dwk, (), (B_wk,))
    ldg(wv_sb[:], dwv, (), (B_wv,))
    ldg(wr_sb[:], dwr, (), (B_wr,))

    def p1_front(i):
        xb = i % 2
        t0 = i * TT
        ld(x_sb[:], dx[:, :, t0:t0 + TT], (), (B_x1,))
        act(xbf[:], x_sb[:], AF.Identity, (B_x1,), (B_xbf,))
        act(x2bf[:], x_sb[:], AF.Square, (B_x1,), (B_x2bf,))
        for c in range(DC):
            mm(4, ones_bf[:], xbf[:, c, :], c == 0, c == DC - 1,
               (B_ones_bf, B_xbf))
        for c in range(DC):
            mm(5, ones_bf[:], x2bf[:, c, :], c == 0, c == DC - 1,
               (B_ones_bf, B_x2bf))
        ln_stats(4, 5, m1, B_m1, sA1, B_sA1, sB1, B_sB1)
        if i == 0:
            vop(lambda e: e.memset(xhat[:, :, 0:1], 0.0), (), (B_xhat,))
        else:
            vop(lambda e: e.tensor_copy(xhat[:, :, 0:1], xhat[:, :, TT:TT + 1]),
                (B_xhat,), (B_xhat,))
        tt(tmpa[:], x_sb[:], bc(m1, DC), ALU.subtract,
           (B_x1, B_m1), (B_tmpa,))
        tt(xhat[:, :, 1:TT + 1], tmpa[:], bc(sB1, DC), ALU.mult,
           (B_tmpa, B_sB1), (B_xhat,))
        tt(tmpa[:], xhat[:, :, 1:TT + 1], xhat[:, :, 0:TT], ALU.subtract,
           (B_xhat,), (B_tmpa,))
        for q, mh, mB in (("k", mk_sb, B_mk), ("v", mv_sb, B_mv),
                          ("r", mr_sb, B_mr)):
            dst = xqbf[(q, xb)]
            dB = B_xqbf[(q, xb)]
            for c in range(DC):
                stt(dst[:, c, :], tmpa[:, c, :], mh[:, c:c + 1],
                    xhat[:, c, 0:TT], ALU.mult, ALU.add,
                    (B_tmpa, mB, B_xhat), (dB,))

    def p1_back(i):
        xb = i % 2
        t0 = i * TT
        g = 0
        for q, wh, wB in (("k", wk_sb, B_wk), ("v", wv_sb, B_wv),
                          ("r", wr_sb, B_wr)):
            src = xqbf[(q, xb)]
            sB = B_xqbf[(q, xb)]
            for j in range(DC):
                bank = g % 4
                g += 1
                for c in range(DC):
                    mm(bank, wh[:, c, j * P:(j + 1) * P], src[:, c, :],
                       c == 0, c == DC - 1, (wB, sB))
                if q == "k":
                    act(E_sb[:, j, :], pb[bank][:], AF.Exp,
                        (pbB[bank],), (B_E,))
                elif q == "v":
                    act(v_sb[:, j, :], pb[bank][:], AF.Identity,
                        (pbB[bank],), (B_v,))
                else:
                    if j == 0:
                        tt(ev_sb[:], E_sb[:], v_sb[:], ALU.mult,
                           (B_E, B_v), (B_ev,))
                    act(v_sb[:, j, :], pb[bank][:], AF.Exp,
                        (pbB[bank],), (B_v,), scale=-1.0)
        st(d_e[:, :, t0:t0 + TT], E_sb[:], (B_E,), (B_de[i],))
        st(d_ev[:, :, t0:t0 + TT], ev_sb[:], (B_ev,), (B_dev[i],))
        for j in range(DC):
            act(m1[:], v_sb[:, j, :], AF.Ln, (B_v,), (B_m1,), bias=1.0)
            act(E_sb[:, j, :], m1[:], AF.Exp, (B_m1,), (B_E,), scale=-1.0)
        st(d_r[:, :, t0:t0 + TT], E_sb[:], (B_E,), (B_dr[i],))
        hs = H // NT
        ldg(d_wgbf[:, :, i * hs:(i + 1) * hs], dwg[:, :, i * hs:(i + 1) * hs],
            (), (B_wgbf,))
        ldg(d_wubf[:, :, i * hs:(i + 1) * hs], dwu[:, :, i * hs:(i + 1) * hs],
            (), (B_wubf,))
        hcs = HC // NT
        ldg(d_wdbf[:, i * hcs:(i + 1) * hcs, :], dwd[:, i * hcs:(i + 1) * hcs, :],
            (), (B_wdbf,))

    p1_front(0)
    for i in range(NT):
        if i + 1 < NT:
            p1_front(i + 1)
        p1_back(i)

    p1.close()
    sch.phase_barrier()

    p2 = ExitStack()
    wo_sb, B_wo = sb(p2, "p2_wo", [P, DC, D], BF16)
    E2 = []
    B_E2 = []
    for b in range(2):
        h, bb = sb(p2, f"p2_E{b}", [P, DC, TT], F32)
        E2.append(h); B_E2.append(bb)
    ev2, B_ev2 = sb(p2, "p2_ev", [P, DC, TT], F32)
    r2, B_r2 = sb(p2, "p2_r", [P, DC, TT], F32)
    x2, B_x2 = sb(p2, "p2_x", [P, DC, TT], F32)
    ba, B_ba = sb(p2, "p2_ba", [P, 2 * DC, TT + 1], F32)
    s1, B_s1 = sb(p2, "p2_s1", [P, DC, TT], F32)
    s2, B_s2 = sb(p2, "p2_s2", [P, DC, TT], F32)
    s3, B_s3 = sb(p2, "p2_s3", [P, DC, TT], F32)
    y2sb, B_y2sb = sb(p2, "p2_y", [P, DC, TT], F32)
    wkvbf, B_wkvbf = sb(p2, "p2_wkvbf", [P, DC, TT], BF16)
    m2, B_m2 = sb(p2, "p2_m", [P, TT], F32)
    sA2, B_sA2 = sb(p2, "p2_sA", [P, TT], F32)
    sB2, B_sB2 = sb(p2, "p2_sB", [P, TT], F32)

    ldg(wo_sb[:], dwo, (), (B_wo,))

    def p2_loads(i):
        t0 = i * TT
        ld(E2[i % 2][:], d_e[:, :, t0:t0 + TT], (B_de[i],), (B_E2[i % 2],))
        ld(ev2[:], d_ev[:, :, t0:t0 + TT], (B_dev[i],), (B_ev2,))

    def p2_scans(i):
        eb = i % 2
        if i == 0:
            sop(lambda e: e.memzero(ba[:, :, 0:1]), (), (B_ba,))
        else:
            sop(lambda e: e.copy(ba[:, :, 0:1], ba[:, :, TT:TT + 1]),
                (B_ba,), (B_ba,))
        for j in range(DC):
            vop(lambda e, j=j, eb=eb: e.tensor_tensor_scan(
                ba[:, j, 1:TT + 1],
                lam_sb[:, j:j + 1].broadcast_to([P, TT]),
                E2[eb][:, j, :], ba[:, j, 0:1], ALU.mult, ALU.add),
                (B_lam, B_E2[eb], B_ba), (B_ba,))
        for j in range(DC):
            vop(lambda e, j=j: e.tensor_tensor_scan(
                ba[:, DC + j, 1:TT + 1],
                lam_sb[:, j:j + 1].broadcast_to([P, TT]),
                ev2[:, j, :], ba[:, DC + j, 0:1], ALU.mult, ALU.add),
                (B_lam, B_ev2, B_ba), (B_ba,))

    def p2_dennum(i):
        eb = i % 2
        for j in range(DC):
            stt(s1[:, j, :], E2[eb][:, j, :], eu_sb[:, j:j + 1],
                ba[:, j, 0:TT], ALU.mult, ALU.add,
                (B_E2[eb], B_eu, B_ba), (B_s1,))
        for j in range(DC):
            stt(s2[:, j, :], ev2[:, j, :], eu_sb[:, j:j + 1],
                ba[:, DC + j, 0:TT], ALU.mult, ALU.add,
                (B_ev2, B_eu, B_ba), (B_s2,))

    def p2_rdenS(i):
        act(E2[i % 2][:], s1[:], AF.Ln, (B_s1,), (B_E2[i % 2],))
        act(s1[:], E2[i % 2][:], AF.Exp, (B_E2[i % 2],), (B_s1,), scale=-1.0)

    def p2_head(i):
        tt(s3[:], s2[:], s1[:], ALU.mult, (B_s2, B_s1), (B_s3,))
        tt(wkvbf[:], s3[:], r2[:], ALU.mult, (B_s3, B_r2), (B_wkvbf,))
        if i + 1 < NT:
            ld(r2[:], d_r[:, :, (i + 1) * TT:(i + 2) * TT],
               (B_dr[i + 1],), (B_r2,))

    def p2_rest(i):
        t0 = i * TT
        for j in range(DC):
            bank = j % 4
            for c in range(DC):
                mm(bank, wo_sb[:, c, j * P:(j + 1) * P], wkvbf[:, c, :],
                   c == 0, c == DC - 1, (B_wo, B_wkvbf))
            act(y2sb[:, j, :], pb[bank][:], AF.Identity,
                (pbB[bank],), (B_y2sb,))
            act(s3[:, j, :], y2sb[:, j, :], AF.Square, (B_y2sb,), (B_s3,))
        if i + 1 < NT:
            p2_rdenS(i + 1)
        for c in range(DC):
            mm(4, ones_f[:], y2sb[:, c, :], c == 0, c == DC - 1,
               (B_ones_f, B_y2sb))
        for c in range(DC):
            mm(5, ones_f[:], s3[:, c, :], c == 0, c == DC - 1,
               (B_ones_f, B_s3))
        ln_stats(4, 5, m2, B_m2, sA2, B_sA2, sB2, B_sB2)
        tt(s3[:], y2sb[:], bc(m2, DC), ALU.subtract, (B_y2sb, B_m2), (B_s3,))
        tt(y2sb[:], s3[:], bc(sB2, DC), ALU.mult, (B_s3, B_sB2), (B_y2sb,))
        tt(s3[:], y2sb[:], x2[:], ALU.add, (B_y2sb, B_x2), (B_s3,))
        st(d_xp[:, :, t0:t0 + TT], s3[:], (B_s3,), (B_dxp[i],))
        if i + 1 < NT:
            ld(x2[:], dx[:, :, (i + 1) * TT:(i + 2) * TT], (), (B_x2,))

    p2_loads(0)
    ld(r2[:], d_r[:, :, 0:TT], (B_dr[0],), (B_r2,))
    ld(x2[:], dx[:, :, 0:TT], (), (B_x2,))
    p2_scans(0)
    p2_dennum(0)
    p2_rdenS(0)
    for i in range(NT):
        p2_head(i)
        if i + 1 < NT:
            p2_loads(i + 1)
            p2_scans(i + 1)
            p2_dennum(i + 1)
        p2_rest(i)

    p2.close()
    sch.phase_barrier()

    p3 = ExitStack()
    wd_sb, B_wd = sb(p3, "p3_wd", [P, HC, D], BF16)
    WIN = 2
    NWB = 3
    wgw = []
    B_wgw = []
    wuw = []
    B_wuw = []
    for b in range(NWB):
        h, bb = sb(p3, f"p3_wg{b}", [P, DC, WIN * P], BF16)
        wgw.append(h); B_wgw.append(bb)
        h, bb = sb(p3, f"p3_wu{b}", [P, DC, WIN * P], BF16)
        wuw.append(h); B_wuw.append(bb)
    xp3 = []
    B_xp3 = []
    for b in range(2):
        h, bb = sb(p3, f"p3_xp{b}", [P, DC, TT], F32)
        xp3.append(h); B_xp3.append(bb)
    xpbf, B_xpbf = sb(p3, "p3_xpbf", [P, DC, TT], BF16)
    xp2bf, B_xp2bf = sb(p3, "p3_xp2bf", [P, DC, TT], BF16)
    xhbf, B_xhbf = sb(p3, "p3_xhbf", [P, DC, TT], BF16)
    hbf, B_hbf = sb(p3, "p3_hbf", [P, HC, TT], BF16)
    sig = []
    B_sig = []
    for b in range(2):
        hs = [sb(p3, f"p3_sig{b}_{k}", [P, TT], F32) for k in range(4)]
        sig.append([a[0] for a in hs]); B_sig.append([a[1] for a in hs])
    y3, B_y3 = sb(p3, "p3_y", [P, DC, TT], F32)
    tmp3, B_tmp3 = sb(p3, "p3_tmp", [P, DC, TT], F32)
    m3, B_m3 = sb(p3, "p3_m3", [P, TT], F32)
    sA3, B_sA3 = sb(p3, "p3_sA3", [P, TT], F32)
    sB3, B_sB3 = sb(p3, "p3_sB3", [P, TT], F32)
    m4, B_m4, sA4, B_sA4, sB4, B_sB4 = m3, B_m3, sA3, B_sA3, sB3, B_sB3

    ld(wd_sb[:], d_wdbf, (B_wdbf,), (B_wd,))

    def p3_front(i):
        xb = i % 2
        t0 = i * TT
        act(xpbf[:], xp3[xb][:], AF.Identity, (B_xp3[xb],), (B_xpbf,))
        act(xp2bf[:], xp3[xb][:], AF.Square, (B_xp3[xb],), (B_xp2bf,))
        for c in range(DC):
            mm(6, ones_bf[:], xpbf[:, c, :], c == 0, c == DC - 1,
               (B_ones_bf, B_xpbf))
        for c in range(DC):
            mm(7, ones_bf[:], xp2bf[:, c, :], c == 0, c == DC - 1,
               (B_ones_bf, B_xp2bf))
        ln_stats(6, 7, m3, B_m3, sA3, B_sA3, sB3, B_sB3)
        tt(tmp3[:], xp3[xb][:], bc(m3, DC), ALU.subtract,
           (B_xp3[xb], B_m3), (B_tmp3,))
        tt(xhbf[:], tmp3[:], bc(sB3, DC), ALU.mult, (B_tmp3, B_sB3), (B_xhbf,))

    def p3_tile(i):
        xb = i % 2
        t0 = i * TT
        for hh in range(HC):
            if hh % WIN == 0:
                wb = (hh // WIN) % NWB
                h0 = hh * P
                ld(wgw[wb][:], d_wgbf[:, :, h0:h0 + WIN * P],
                   (B_wgbf,), (B_wgw[wb],))
                ld(wuw[wb][:], d_wubf[:, :, h0:h0 + WIN * P],
                   (B_wubf,), (B_wuw[wb],))
            wb = (hh // WIN) % NWB
            hcol = (hh % WIN) * P
            bg = hh % 2
            bu = 2 + hh % 2
            for c in range(DC):
                mm(bg, wgw[wb][:, c, hcol:hcol + P], xhbf[:, c, :],
                   c == 0, c == DC - 1, (B_wgw[wb], B_xhbf))
            for c in range(DC):
                mm(bu, wuw[wb][:, c, hcol:hcol + P], xhbf[:, c, :],
                   c == 0, c == DC - 1, (B_wuw[wb], B_xhbf))
            sb_ = hh % 2
            s0_h, s1_h, s2_h, s3_h = sig[sb_]
            s0B, s1B, s2B, s3B = B_sig[sb_]
            act(s0_h[:], pb[bg][:], AF.Relu, (pbB[bg],), (s0B,),
                scale=SF, bias=30.0)
            act(s1_h[:], s0_h[:], AF.Exp, (s0B,), (s1B,), scale=-1.0,
                bias=30.0)
            act(s0_h[:], s1_h[:], AF.Ln, (s1B,), (s0B,), bias=1.0)
            act(s1_h[:], s0_h[:], AF.Exp, (s0B,), (s1B,), scale=-1.0)
            act(s2_h[:], pb[bu][:], AF.Identity, (pbB[bu],), (s2B,), scale=SF)
            act(s3_h[:], pb[bg][:], AF.Identity, (pbB[bg],), (s3B,))
            tt(s0_h[:], s3_h[:], s2_h[:], ALU.mult, (s3B, s2B), (s0B,))
            stt(hbf[:, hh, :], s0_h[:], SF, s1_h[:], ALU.mult, ALU.mult,
                (s0B, s1B), (B_hbf,))
        if i + 1 < NT:
            ld(xp3[(i + 1) % 2][:], d_xp[:, :, (i + 1) * TT:(i + 2) * TT],
               (B_dxp[i + 1],), (B_xp3[(i + 1) % 2],))
            p3_front(i + 1)
        for j in range(DC):
            bank = 4 + j % 2
            for hc_ in range(HC):
                mm(bank, wd_sb[:, hc_, j * P:(j + 1) * P], hbf[:, hc_, :],
                   hc_ == 0, hc_ == HC - 1, (B_wd, B_hbf))
            act(y3[:, j, :], pb[bank][:], AF.Identity, (pbB[bank],), (B_y3,),
                scale=SF)
            act(xpbf[:, j, :], y3[:, j, :], AF.Identity, (B_y3,), (B_xpbf,))
            act(xp2bf[:, j, :], y3[:, j, :], AF.Square, (B_y3,), (B_xp2bf,))
        for c in range(DC):
            mm(6, ones_bf[:], xpbf[:, c, :], c == 0, c == DC - 1,
               (B_ones_bf, B_xpbf))
        for c in range(DC):
            mm(7, ones_bf[:], xp2bf[:, c, :], c == 0, c == DC - 1,
               (B_ones_bf, B_xp2bf))
        ln_stats(6, 7, m4, B_m4, sA4, B_sA4, sB4, B_sB4)
        tt(tmp3[:], y3[:], bc(m4, DC), ALU.subtract, (B_y3, B_m4), (B_tmp3,))
        tt(y3[:], tmp3[:], bc(sB4, DC), ALU.mult, (B_tmp3, B_sB4), (B_y3,))
        tt(tmp3[:], y3[:], xp3[xb][:], ALU.add, (B_y3, B_xp3[xb]), (B_tmp3,))
        st(dout[:, :, t0:t0 + TT], tmp3[:], (B_tmp3,), inc=True)

    ld(xp3[0][:], d_xp[:, :, 0:TT], (B_dxp[0],), (B_xp3[0],))
    p3_front(0)
    for i in range(NT):
        p3_tile(i)

    if debug:
        dbg_h = nc.dram_tensor("dbg_h", [P, HC, TT], F32,
                               kind="ExternalOutput").ap()
        dbg_y = nc.dram_tensor("dbg_y", [P, DC, TT], F32,
                               kind="ExternalOutput").ap()
        dbg_xh = nc.dram_tensor("dbg_xh", [P, DC, TT], F32,
                                kind="ExternalOutput").ap()
        st(dbg_h, hbf[:], (B_hbf,), inc=True)
        st(dbg_y, y3[:], (B_y3,), inc=True)
        st(dbg_xh, xhbf[:], (B_xhbf,), inc=True)

    p3.close()

    totals = sch.finalize()
    with nc.Block() as block:
        @block.tensor
        def _(e):
            sch.emit_stream("tensor", e, sems)

        @block.vector
        def _(e):
            sch.emit_stream("vector", e, sems)

        @block.scalar
        def _(e):
            sch.emit_stream("scalar", e, sems)

        @block.sync
        def _(e):
            sch.emit_stream("syq", e, sems)
            for lane in range(NLANES):
                t = totals.get(f"syq{lane}", 0)
                if t:
                    e.wait_ge(sems[f"syq{lane}"], t)

        @block.gpsimd
        def _(e):
            sch.emit_stream("gpq", e, sems)
            for lane in range(NLANES):
                t = totals.get(f"gpq{lane}", 0)
                if t:
                    e.wait_ge(sems[f"gpq{lane}"], t)

    es_all.close()
    pers.close()
    return nc


_NC_CACHE = None


def _get_nc():
    global _NC_CACHE
    if _NC_CACHE is None:
        _NC_CACHE = build_nc()
    return _NC_CACHE


def _fm_vec(v):
    return np.ascontiguousarray(np.asarray(v, np.float32).reshape(DC, P).T)


def _fm_mat(w):
    w = np.asarray(w)
    return np.ascontiguousarray(
        w.reshape(w.shape[0] // P, P, w.shape[1]).transpose(1, 0, 2))


def kernel(L0_attn_x, L0_attn_time_decay, L0_attn_time_first,
           L0_attn_time_mix_k, L0_attn_time_mix_v, L0_attn_time_mix_r,
           L0_attn_W_key, L0_attn_W_value, L0_attn_W_receptance,
           L0_attn_W_output, L0_ffn_w_gate, L0_ffn_w_up, L0_ffn_w_down):
    x = np.asarray(L0_attn_x, np.float32)
    Bsz = x.shape[0]
    base = {
        "wkt": _fm_mat(np.ascontiguousarray(np.asarray(L0_attn_W_key, np.float32).T)),
        "wvt": _fm_mat(np.ascontiguousarray(np.asarray(L0_attn_W_value, np.float32).T)),
        "wrt": _fm_mat(np.ascontiguousarray(np.asarray(L0_attn_W_receptance, np.float32).T)),
        "wot": _fm_mat(np.ascontiguousarray(np.asarray(L0_attn_W_output, np.float32).T)),
        "wgt": _fm_mat(np.ascontiguousarray(np.asarray(L0_ffn_w_gate, np.int32).T)),
        "wut": _fm_mat(np.ascontiguousarray(np.asarray(L0_ffn_w_up, np.int32).T)),
        "wdt": _fm_mat(np.ascontiguousarray(np.asarray(L0_ffn_w_down, np.int32).T)),
        "td": _fm_vec(L0_attn_time_decay),
        "tf": _fm_vec(L0_attn_time_first),
        "mk": _fm_vec(np.asarray(L0_attn_time_mix_k).reshape(-1)),
        "mv": _fm_vec(np.asarray(L0_attn_time_mix_v).reshape(-1)),
        "mr": _fm_vec(np.asarray(L0_attn_time_mix_r).reshape(-1)),
    }
    in_maps = []
    for b in range(Bsz):
        m = dict(base)
        m["xt"] = np.ascontiguousarray(
            x[b].T.reshape(DC, P, TSEQ).transpose(1, 0, 2))
        in_maps.append(m)

    nc = _get_nc()
    res = run_bass_kernel_spmd(nc, in_maps, core_ids=list(range(Bsz)))
    out = np.empty((Bsz, TSEQ, D), np.float32)
    for b in range(Bsz):
        o = np.asarray(res.results[b]["out"])
        out[b] = o.transpose(1, 0, 2).reshape(D, TSEQ).T
    return out


# version 28
# speedup vs baseline: 1.0169x; 1.0169x over previous
import numpy as np
import ml_dtypes
from contextlib import ExitStack

import concourse.bass as bass
import concourse.mybir as mybir
from concourse.bass_utils import run_bass_kernel_spmd

F32 = mybir.dt.float32
BF16 = mybir.dt.bfloat16
I32 = mybir.dt.int32
AF = mybir.ActivationFunctionType
ALU = mybir.AluOpType

P = 128
D = 1024
DC = D // P
TSEQ = 4096
TT = 512
NT = TSEQ // TT
H = 3072
HC = H // P
SF = 1.0 / 127.0
EPS = 1e-5
DINV = 1.0 / D

ENGINES = ("tensor", "vector", "scalar", "syq", "gpq")


class Buf:
    __slots__ = ("name", "writer", "readers")

    def __init__(self, name):
        self.name = name
        self.writer = None
        self.readers = []


NLANES = 8


class Sched:
    def __init__(self):
        self.per = {e: [] for e in ENGINES}
        self.qcount = {e: 0 for e in ENGINES}
        self.nops = 0
        self.barrier = []
        self.last_by_sem = {}

    def add(self, eng, emit, reads=(), writes=(), dma=False, force_inc=False):
        op = {"eng": eng, "emit": emit, "dma": dma, "deps": None,
              "inc": force_inc or dma, "val": None, "sem": eng}
        if dma:
            k = self.qcount[eng]
            self.qcount[eng] = k + 1
            op["sem"] = f"{eng}{k % NLANES}"
            op["val"] = 16 * (k // NLANES + 1)
        self.nops += 1
        depmap = {}
        for b in self.barrier:
            depmap[id(b)] = b
        for b in reads:
            if b.writer is not None:
                depmap[id(b.writer)] = b.writer
            b.readers.append(op)
        for b in writes:
            if b.writer is not None:
                depmap[id(b.writer)] = b.writer
            for r in b.readers:
                depmap[id(r)] = r
            b.writer = op
            b.readers = []
        op["deps"] = [d for d in depmap.values() if d is not op]
        self.per[eng].append(op)
        self.last_by_sem[op["sem"]] = op
        return op

    def phase_barrier(self):
        self.barrier = list(self.last_by_sem.values())

    def finalize(self):
        for e in ENGINES:
            for op in self.per[e]:
                for d in op["deps"]:
                    if d["eng"] != op["eng"]:
                        d["inc"] = True
        totals = {}
        for e in ENGINES:
            cnt = 0
            for op in self.per[e]:
                if op["inc"] and not op["dma"]:
                    cnt += 1
                    op["val"] = cnt
            totals[e] = cnt
        for e in ENGINES:
            n = self.qcount[e]
            for lane in range(min(n, NLANES)):
                cnt_lane = (n - lane + NLANES - 1) // NLANES
                totals[f"{e}{lane}"] = 16 * cnt_lane
        return totals

    def emit_stream(self, eng, eng_obj, sems):
        watermark = {}
        for op in self.per[eng]:
            waits = {}
            for d in op["deps"]:
                if d["eng"] == eng or d["val"] is None:
                    continue
                if d["val"] > waits.get(d["sem"], 0):
                    waits[d["sem"]] = d["val"]
            for de in sorted(waits):
                v = waits[de]
                if v > watermark.get(de, 0):
                    eng_obj.wait_ge(sems[de], v)
                    watermark[de] = v
            instr = op["emit"](eng_obj)
            if op["inc"]:
                instr.then_inc(sems[op["sem"]], 16 if op["dma"] else 1)


def build_nc(debug=False):
    nc = bass.Bass("TRN2", target_bir_lowering=False,
                   detect_race_conditions=False)
    sch = Sched()
    scratch_kind = "ExternalOutput" if debug else "Internal"

    dx = nc.dram_tensor("xt", [P, DC, TSEQ], F32, kind="ExternalInput").ap()
    dwk = nc.dram_tensor("wkt", [P, DC, D], F32, kind="ExternalInput").ap()
    dwv = nc.dram_tensor("wvt", [P, DC, D], F32, kind="ExternalInput").ap()
    dwr = nc.dram_tensor("wrt", [P, DC, D], F32, kind="ExternalInput").ap()
    dwo = nc.dram_tensor("wot", [P, DC, D], F32, kind="ExternalInput").ap()
    dwg = nc.dram_tensor("wgt", [P, DC, H], I32, kind="ExternalInput").ap()
    dwu = nc.dram_tensor("wut", [P, DC, H], I32, kind="ExternalInput").ap()
    dwd = nc.dram_tensor("wdt", [P, HC, D], I32, kind="ExternalInput").ap()
    dtd = nc.dram_tensor("td", [P, DC], F32, kind="ExternalInput").ap()
    dtf = nc.dram_tensor("tf", [P, DC], F32, kind="ExternalInput").ap()
    dmk = nc.dram_tensor("mk", [P, DC], F32, kind="ExternalInput").ap()
    dmv = nc.dram_tensor("mv", [P, DC], F32, kind="ExternalInput").ap()
    dmr = nc.dram_tensor("mr", [P, DC], F32, kind="ExternalInput").ap()
    dout = nc.dram_tensor("out", [P, DC, TSEQ], F32, kind="ExternalOutput").ap()
    d_e = nc.dram_tensor("e_h", [P, DC, TSEQ], F32, kind=scratch_kind).ap()
    d_ev = nc.dram_tensor("ev_h", [P, DC, TSEQ], F32, kind=scratch_kind).ap()
    d_r = nc.dram_tensor("r_h", [P, DC, TSEQ], F32, kind=scratch_kind).ap()
    d_xp = nc.dram_tensor("xp_h", [P, DC, TSEQ], F32, kind=scratch_kind).ap()
    d_wgbf = nc.dram_tensor("wg_bf", [P, DC, H], BF16, kind="Internal").ap()
    d_wubf = nc.dram_tensor("wu_bf", [P, DC, H], BF16, kind="Internal").ap()
    d_wdbf = nc.dram_tensor("wd_bf", [P, HC, D], BF16, kind="Internal").ap()
    B_wgbf = Buf("d_wgbf"); B_wubf = Buf("d_wubf"); B_wdbf = Buf("d_wdbf")
    B_de = [Buf(f"d_e{i}") for i in range(NT)]
    B_dev = [Buf(f"d_ev{i}") for i in range(NT)]
    B_dr = [Buf(f"d_r{i}") for i in range(NT)]
    B_dxp = [Buf(f"d_xp{i}") for i in range(NT)]

    es_all = ExitStack()
    sems = {e: es_all.enter_context(nc.semaphore(f"sem_{e}")) for e in ENGINES}
    for e in ("syq", "gpq"):
        for lane in range(NLANES):
            sems[f"{e}{lane}"] = es_all.enter_context(
                nc.semaphore(f"sem_{e}{lane}"))

    pb = [es_all.enter_context(nc.psum_tensor(f"pb{i}", [P, TT], F32))
          for i in range(8)]
    pbB = [Buf(f"pb{i}") for i in range(8)]

    def sb(es, name, shape, dtype):
        h = es.enter_context(nc.sbuf_tensor(name, shape, dtype))
        return h, Buf(name)

    pers = ExitStack()
    ones_bf, B_ones_bf = sb(pers, "s_ones_bf", [P, P], BF16)
    ones_f, B_ones_f = sb(pers, "s_ones_f", [P, P], F32)
    lam_sb, B_lam = sb(pers, "s_lam", [P, DC], F32)
    eu_sb, B_eu = sb(pers, "s_eu", [P, DC], F32)
    mk_sb, B_mk = sb(pers, "s_mk", [P, DC], F32)
    mv_sb, B_mv = sb(pers, "s_mv", [P, DC], F32)
    mr_sb, B_mr = sb(pers, "s_mr", [P, DC], F32)
    vtmp, B_vtmp = sb(pers, "s_vtmp", [P, DC], F32)
    cst, B_cst = sb(pers, "s_cst", [P, 3], F32)
    CONST_COL = {EPS: 0, 1.0: 1, 30.0: 2}

    def ld(dst_ap, src_ap, reads, writes, inc=False):
        return sch.add("syq", lambda e, o=dst_ap, i=src_ap: e.dma_start(o, i),
                       reads=reads, writes=writes, dma=True, force_inc=inc)

    def ldg(dst_ap, src_ap, reads, writes, inc=False):
        return sch.add("gpq", lambda e, o=dst_ap, i=src_ap: e.dma_start(o, i),
                       reads=reads, writes=writes, dma=True, force_inc=inc)

    def st(dst_ap, src_ap, reads, writes=(), inc=False):
        return sch.add("gpq", lambda e, o=dst_ap, i=src_ap: e.dma_start(o, i),
                       reads=reads, writes=writes, dma=True, force_inc=inc)

    def top(emit, reads, writes):
        return sch.add("tensor", emit, reads, writes)

    def vop(emit, reads, writes):
        return sch.add("vector", emit, reads, writes)

    def sop(emit, reads, writes):
        return sch.add("scalar", emit, reads, writes)

    def mm(pbank_i, lhsT, rhs, start, stop, reads):
        return top(lambda e, o=pb[pbank_i][:], l=lhsT, r=rhs, a=start, b=stop:
                   e.matmul(o, l, r, start=a, stop=b),
                   reads=reads, writes=(pbB[pbank_i],))

    def act(out_ap, in_ap, func, reads, writes, scale=1.0, bias=0.0):
        if isinstance(bias, float) and bias != 0.0:
            col = CONST_COL[bias]
            bias = cst[:, col:col + 1]
            reads = tuple(reads) + (B_cst,)
        return sop(lambda e, o=out_ap, i=in_ap, f=func, s=scale, b=bias:
                   e.activation(o, i, f, bias=b, scale=s),
                   reads=reads, writes=writes)

    def tt(out_ap, a_ap, b_ap, op, reads, writes):
        return vop(lambda e, o=out_ap, x=a_ap, y=b_ap, p=op:
                   e.tensor_tensor(o, x, y, p), reads, writes)

    def stt(out_ap, in0_ap, scal_ap, in1_ap, op0, op1, reads, writes):
        return vop(lambda e, o=out_ap, i0=in0_ap, s=scal_ap, i1=in1_ap,
                   p0=op0, p1=op1: e.scalar_tensor_tensor(o, i0, s, i1, p0, p1),
                   reads, writes)

    def tbl(e):
        try:
            from concourse.hw_specs import get_activation_tables
            idx = list(get_activation_tables(e.bass.m.arch).keys()
                       ).index("natural_log_exp_and_others")
        except Exception:
            idx = 6
        return e.add_instruction(mybir.InstLoadActFuncSet(
            name=e.bass.get_next_instruction_name(),
            act_func_set_id=idx, ins=[], outs=[]))
    sch.add("scalar", tbl, (), ())

    for val, col in CONST_COL.items():
        vop(lambda e, c=col, v=val: e.memset(cst[:, c:c + 1], v),
            (), (B_cst,))
    vop(lambda e: e.memset(ones_bf[:], 1.0), (), (B_ones_bf,))
    vop(lambda e: e.memset(ones_f[:], 1.0), (), (B_ones_f,))
    ld(mk_sb[:], dmk, (), (B_mk,))
    ld(mv_sb[:], dmv, (), (B_mv,))
    ld(mr_sb[:], dmr, (), (B_mr,))
    ld(vtmp[:], dtd, (), (B_vtmp,))
    act(lam_sb[:], vtmp[:], AF.Exp, (B_vtmp,), (B_lam,))
    act(lam_sb[:], lam_sb[:], AF.Exp, (B_lam,), (B_lam,), scale=-1.0)
    ld(vtmp[:], dtf, (B_lam,), (B_vtmp,))
    act(eu_sb[:], vtmp[:], AF.Exp, (B_vtmp,), (B_eu,))

    def ln_stats(pbS, pbQ, m_h, m_B, sA_h, sA_B, sB_h, sB_B):
        act(m_h[:], pb[pbS][:], AF.Identity, (pbB[pbS],), (m_B,), scale=DINV)
        act(sA_h[:], m_h[:], AF.Square, (m_B,), (sA_B,))
        stt(sB_h[:], pb[pbQ][:], DINV, sA_h[:], ALU.mult, ALU.subtract,
            (pbB[pbQ], sA_B), (sB_B,))
        act(sA_h[:], sB_h[:], AF.Ln, (sB_B,), (sA_B,), bias=EPS)
        act(sB_h[:], sA_h[:], AF.Exp, (sA_B,), (sB_B,), scale=-0.5)

    def bc(h, n_mid):
        return h[:].unsqueeze(1).broadcast_to([P, n_mid, TT])

    p1 = ExitStack()
    wk_sb, B_wk = sb(p1, "p1_wk", [P, DC, D], BF16)
    wv_sb, B_wv = sb(p1, "p1_wv", [P, DC, D], BF16)
    wr_sb, B_wr = sb(p1, "p1_wr", [P, DC, D], BF16)
    x_sb, B_x1 = sb(p1, "p1_x", [P, DC, TT], F32)
    xbf, B_xbf = sb(p1, "p1_xbf", [P, DC, TT], BF16)
    x2bf, B_x2bf = sb(p1, "p1_x2bf", [P, DC, TT], BF16)
    xhat, B_xhat = sb(p1, "p1_xhat", [P, DC, TT + 1], BF16)
    tmpa, B_tmpa = sb(p1, "p1_tmpa", [P, DC, TT], BF16)
    xqbf = {}
    B_xqbf = {}
    for q in ("k", "v", "r"):
        for b in range(2):
            h, bb = sb(p1, f"p1_x{q}bf{b}", [P, DC, TT], BF16)
            xqbf[(q, b)] = h; B_xqbf[(q, b)] = bb
    E_sb, B_E = sb(p1, "p1_E", [P, DC, TT], F32)
    v_sb, B_v = sb(p1, "p1_v", [P, DC, TT], F32)
    ev_sb, B_ev = sb(p1, "p1_ev", [P, DC, TT], F32)
    m1, B_m1 = sb(p1, "p1_m", [P, TT], F32)
    sA1, B_sA1 = sb(p1, "p1_sA", [P, TT], F32)
    sB1, B_sB1 = sb(p1, "p1_sB", [P, TT], F32)

    ldg(wk_sb[:], dwk, (), (B_wk,))
    ldg(wv_sb[:], dwv, (), (B_wv,))
    ldg(wr_sb[:], dwr, (), (B_wr,))

    def p1_front(i):
        xb = i % 2
        t0 = i * TT
        ld(x_sb[:], dx[:, :, t0:t0 + TT], (), (B_x1,))
        act(xbf[:], x_sb[:], AF.Identity, (B_x1,), (B_xbf,))
        act(x2bf[:], x_sb[:], AF.Square, (B_x1,), (B_x2bf,))
        for c in range(DC):
            mm(4, ones_bf[:], xbf[:, c, :], c == 0, c == DC - 1,
               (B_ones_bf, B_xbf))
        for c in range(DC):
            mm(5, ones_bf[:], x2bf[:, c, :], c == 0, c == DC - 1,
               (B_ones_bf, B_x2bf))
        ln_stats(4, 5, m1, B_m1, sA1, B_sA1, sB1, B_sB1)
        if i == 0:
            vop(lambda e: e.memset(xhat[:, :, 0:1], 0.0), (), (B_xhat,))
        else:
            vop(lambda e: e.tensor_copy(xhat[:, :, 0:1], xhat[:, :, TT:TT + 1]),
                (B_xhat,), (B_xhat,))
        tt(tmpa[:], x_sb[:], bc(m1, DC), ALU.subtract,
           (B_x1, B_m1), (B_tmpa,))
        tt(xhat[:, :, 1:TT + 1], tmpa[:], bc(sB1, DC), ALU.mult,
           (B_tmpa, B_sB1), (B_xhat,))
        tt(tmpa[:], xhat[:, :, 1:TT + 1], xhat[:, :, 0:TT], ALU.subtract,
           (B_xhat,), (B_tmpa,))
        for q, mh, mB in (("k", mk_sb, B_mk), ("v", mv_sb, B_mv),
                          ("r", mr_sb, B_mr)):
            dst = xqbf[(q, xb)]
            dB = B_xqbf[(q, xb)]
            for c in range(DC):
                stt(dst[:, c, :], tmpa[:, c, :], mh[:, c:c + 1],
                    xhat[:, c, 0:TT], ALU.mult, ALU.add,
                    (B_tmpa, mB, B_xhat), (dB,))

    def p1_back(i):
        xb = i % 2
        t0 = i * TT
        g = 0
        for q, wh, wB in (("k", wk_sb, B_wk), ("v", wv_sb, B_wv),
                          ("r", wr_sb, B_wr)):
            src = xqbf[(q, xb)]
            sB = B_xqbf[(q, xb)]
            for j in range(DC):
                bank = (0, 1, 2, 3, 6, 7)[g % 6]
                g += 1
                for c in range(DC):
                    mm(bank, wh[:, c, j * P:(j + 1) * P], src[:, c, :],
                       c == 0, c == DC - 1, (wB, sB))
                if q == "k":
                    act(E_sb[:, j, :], pb[bank][:], AF.Exp,
                        (pbB[bank],), (B_E,))
                elif q == "v":
                    act(v_sb[:, j, :], pb[bank][:], AF.Identity,
                        (pbB[bank],), (B_v,))
                else:
                    if j == 0:
                        tt(ev_sb[:], E_sb[:], v_sb[:], ALU.mult,
                           (B_E, B_v), (B_ev,))
                    act(v_sb[:, j, :], pb[bank][:], AF.Exp,
                        (pbB[bank],), (B_v,), scale=-1.0)
        st(d_e[:, :, t0:t0 + TT], E_sb[:], (B_E,), (B_de[i],))
        st(d_ev[:, :, t0:t0 + TT], ev_sb[:], (B_ev,), (B_dev[i],))
        for j in range(DC):
            act(m1[:], v_sb[:, j, :], AF.Ln, (B_v,), (B_m1,), bias=1.0)
            act(E_sb[:, j, :], m1[:], AF.Exp, (B_m1,), (B_E,), scale=-1.0)
        st(d_r[:, :, t0:t0 + TT], E_sb[:], (B_E,), (B_dr[i],))
        hs = H // NT
        ldg(d_wgbf[:, :, i * hs:(i + 1) * hs], dwg[:, :, i * hs:(i + 1) * hs],
            (), (B_wgbf,))
        ldg(d_wubf[:, :, i * hs:(i + 1) * hs], dwu[:, :, i * hs:(i + 1) * hs],
            (), (B_wubf,))
        hcs = HC // NT
        ldg(d_wdbf[:, i * hcs:(i + 1) * hcs, :], dwd[:, i * hcs:(i + 1) * hcs, :],
            (), (B_wdbf,))

    p1_front(0)
    for i in range(NT):
        if i + 1 < NT:
            p1_front(i + 1)
        p1_back(i)

    p1.close()
    sch.phase_barrier()

    p2 = ExitStack()
    wo_sb, B_wo = sb(p2, "p2_wo", [P, DC, D], BF16)
    E2 = []
    B_E2 = []
    for b in range(2):
        h, bb = sb(p2, f"p2_E{b}", [P, DC, TT], F32)
        E2.append(h); B_E2.append(bb)
    ev2, B_ev2 = sb(p2, "p2_ev", [P, DC, TT], F32)
    r2, B_r2 = sb(p2, "p2_r", [P, DC, TT], F32)
    x2, B_x2 = sb(p2, "p2_x", [P, DC, TT], F32)
    ba, B_ba = sb(p2, "p2_ba", [P, 2 * DC, TT + 1], F32)
    s1, B_s1 = sb(p2, "p2_s1", [P, DC, TT], F32)
    s2, B_s2 = sb(p2, "p2_s2", [P, DC, TT], F32)
    s3, B_s3 = sb(p2, "p2_s3", [P, DC, TT], F32)
    y2sb, B_y2sb = sb(p2, "p2_y", [P, DC, TT], F32)
    wkvbf, B_wkvbf = sb(p2, "p2_wkvbf", [P, DC, TT], BF16)
    m2, B_m2 = sb(p2, "p2_m", [P, TT], F32)
    sA2, B_sA2 = sb(p2, "p2_sA", [P, TT], F32)
    sB2, B_sB2 = sb(p2, "p2_sB", [P, TT], F32)

    ldg(wo_sb[:], dwo, (), (B_wo,))

    def p2_loads(i):
        t0 = i * TT
        ld(E2[i % 2][:], d_e[:, :, t0:t0 + TT], (B_de[i],), (B_E2[i % 2],))
        ld(ev2[:], d_ev[:, :, t0:t0 + TT], (B_dev[i],), (B_ev2,))

    def p2_scans(i):
        eb = i % 2
        if i == 0:
            sop(lambda e: e.memzero(ba[:, :, 0:1]), (), (B_ba,))
        else:
            sop(lambda e: e.copy(ba[:, :, 0:1], ba[:, :, TT:TT + 1]),
                (B_ba,), (B_ba,))
        for j in range(DC):
            vop(lambda e, j=j, eb=eb: e.tensor_tensor_scan(
                ba[:, j, 1:TT + 1],
                lam_sb[:, j:j + 1].broadcast_to([P, TT]),
                E2[eb][:, j, :], ba[:, j, 0:1], ALU.mult, ALU.add),
                (B_lam, B_E2[eb], B_ba), (B_ba,))
        for j in range(DC):
            vop(lambda e, j=j: e.tensor_tensor_scan(
                ba[:, DC + j, 1:TT + 1],
                lam_sb[:, j:j + 1].broadcast_to([P, TT]),
                ev2[:, j, :], ba[:, DC + j, 0:1], ALU.mult, ALU.add),
                (B_lam, B_ev2, B_ba), (B_ba,))

    def p2_dennum(i):
        eb = i % 2
        for j in range(DC):
            stt(s1[:, j, :], E2[eb][:, j, :], eu_sb[:, j:j + 1],
                ba[:, j, 0:TT], ALU.mult, ALU.add,
                (B_E2[eb], B_eu, B_ba), (B_s1,))
        for j in range(DC):
            stt(s2[:, j, :], ev2[:, j, :], eu_sb[:, j:j + 1],
                ba[:, DC + j, 0:TT], ALU.mult, ALU.add,
                (B_ev2, B_eu, B_ba), (B_s2,))

    def p2_rdenS(i):
        act(E2[i % 2][:], s1[:], AF.Ln, (B_s1,), (B_E2[i % 2],))
        act(s1[:], E2[i % 2][:], AF.Exp, (B_E2[i % 2],), (B_s1,), scale=-1.0)

    def p2_head(i):
        tt(s3[:], s2[:], s1[:], ALU.mult, (B_s2, B_s1), (B_s3,))
        tt(wkvbf[:], s3[:], r2[:], ALU.mult, (B_s3, B_r2), (B_wkvbf,))
        if i + 1 < NT:
            ld(r2[:], d_r[:, :, (i + 1) * TT:(i + 2) * TT],
               (B_dr[i + 1],), (B_r2,))

    def p2_rest(i):
        t0 = i * TT
        for j in range(DC):
            bank = (0, 1, 2, 3, 6, 7)[j % 6]
            for c in range(DC):
                mm(bank, wo_sb[:, c, j * P:(j + 1) * P], wkvbf[:, c, :],
                   c == 0, c == DC - 1, (B_wo, B_wkvbf))
            act(y2sb[:, j, :], pb[bank][:], AF.Identity,
                (pbB[bank],), (B_y2sb,))
        act(s3[:], y2sb[:], AF.Square, (B_y2sb,), (B_s3,))
        if i + 1 < NT:
            p2_rdenS(i + 1)
        for c in range(DC):
            mm(4, ones_f[:], y2sb[:, c, :], c == 0, c == DC - 1,
               (B_ones_f, B_y2sb))
        for c in range(DC):
            mm(5, ones_f[:], s3[:, c, :], c == 0, c == DC - 1,
               (B_ones_f, B_s3))
        ln_stats(4, 5, m2, B_m2, sA2, B_sA2, sB2, B_sB2)
        tt(s3[:], y2sb[:], bc(m2, DC), ALU.subtract, (B_y2sb, B_m2), (B_s3,))
        tt(y2sb[:], s3[:], bc(sB2, DC), ALU.mult, (B_s3, B_sB2), (B_y2sb,))
        tt(s3[:], y2sb[:], x2[:], ALU.add, (B_y2sb, B_x2), (B_s3,))
        st(d_xp[:, :, t0:t0 + TT], s3[:], (B_s3,), (B_dxp[i],))
        if i + 1 < NT:
            ld(x2[:], dx[:, :, (i + 1) * TT:(i + 2) * TT], (), (B_x2,))

    p2_loads(0)
    ld(r2[:], d_r[:, :, 0:TT], (B_dr[0],), (B_r2,))
    ld(x2[:], dx[:, :, 0:TT], (), (B_x2,))
    p2_scans(0)
    p2_dennum(0)
    p2_rdenS(0)
    for i in range(NT):
        p2_head(i)
        if i + 1 < NT:
            p2_loads(i + 1)
            p2_scans(i + 1)
            p2_dennum(i + 1)
        p2_rest(i)

    p2.close()
    sch.phase_barrier()

    p3 = ExitStack()
    wd_sb, B_wd = sb(p3, "p3_wd", [P, HC, D], BF16)
    WIN = 2
    NWB = 3
    wgw = []
    B_wgw = []
    wuw = []
    B_wuw = []
    for b in range(NWB):
        h, bb = sb(p3, f"p3_wg{b}", [P, DC, WIN * P], BF16)
        wgw.append(h); B_wgw.append(bb)
        h, bb = sb(p3, f"p3_wu{b}", [P, DC, WIN * P], BF16)
        wuw.append(h); B_wuw.append(bb)
    xp3 = []
    B_xp3 = []
    for b in range(2):
        h, bb = sb(p3, f"p3_xp{b}", [P, DC, TT], F32)
        xp3.append(h); B_xp3.append(bb)
    xpbf, B_xpbf = sb(p3, "p3_xpbf", [P, DC, TT], BF16)
    xp2bf, B_xp2bf = sb(p3, "p3_xp2bf", [P, DC, TT], BF16)
    xhbf, B_xhbf = sb(p3, "p3_xhbf", [P, DC, TT], BF16)
    hbf, B_hbf = sb(p3, "p3_hbf", [P, HC, TT], BF16)
    sig = []
    B_sig = []
    for b in range(2):
        hs = [sb(p3, f"p3_sig{b}_{k}", [P, TT], F32) for k in range(4)]
        sig.append([a[0] for a in hs]); B_sig.append([a[1] for a in hs])
    y3, B_y3 = sb(p3, "p3_y", [P, DC, TT], F32)
    tmp3, B_tmp3 = sb(p3, "p3_tmp", [P, DC, TT], F32)
    m3, B_m3 = sb(p3, "p3_m3", [P, TT], F32)
    sA3, B_sA3 = sb(p3, "p3_sA3", [P, TT], F32)
    sB3, B_sB3 = sb(p3, "p3_sB3", [P, TT], F32)
    m4, B_m4, sA4, B_sA4, sB4, B_sB4 = m3, B_m3, sA3, B_sA3, sB3, B_sB3

    ld(wd_sb[:], d_wdbf, (B_wdbf,), (B_wd,))

    def p3_front(i):
        xb = i % 2
        t0 = i * TT
        act(xpbf[:], xp3[xb][:], AF.Identity, (B_xp3[xb],), (B_xpbf,))
        act(xp2bf[:], xp3[xb][:], AF.Square, (B_xp3[xb],), (B_xp2bf,))
        for c in range(DC):
            mm(6, ones_bf[:], xpbf[:, c, :], c == 0, c == DC - 1,
               (B_ones_bf, B_xpbf))
        for c in range(DC):
            mm(7, ones_bf[:], xp2bf[:, c, :], c == 0, c == DC - 1,
               (B_ones_bf, B_xp2bf))
        ln_stats(6, 7, m3, B_m3, sA3, B_sA3, sB3, B_sB3)
        tt(tmp3[:], xp3[xb][:], bc(m3, DC), ALU.subtract,
           (B_xp3[xb], B_m3), (B_tmp3,))
        tt(xhbf[:], tmp3[:], bc(sB3, DC), ALU.mult, (B_tmp3, B_sB3), (B_xhbf,))

    def p3_tile(i):
        xb = i % 2
        t0 = i * TT
        for hh in range(HC):
            if hh % WIN == 0:
                wb = (hh // WIN) % NWB
                h0 = hh * P
                ld(wgw[wb][:], d_wgbf[:, :, h0:h0 + WIN * P],
                   (B_wgbf,), (B_wgw[wb],))
                ld(wuw[wb][:], d_wubf[:, :, h0:h0 + WIN * P],
                   (B_wubf,), (B_wuw[wb],))
            wb = (hh // WIN) % NWB
            hcol = (hh % WIN) * P
            bg = hh % 2
            bu = 2 + hh % 2
            for c in range(DC):
                mm(bg, wgw[wb][:, c, hcol:hcol + P], xhbf[:, c, :],
                   c == 0, c == DC - 1, (B_wgw[wb], B_xhbf))
            for c in range(DC):
                mm(bu, wuw[wb][:, c, hcol:hcol + P], xhbf[:, c, :],
                   c == 0, c == DC - 1, (B_wuw[wb], B_xhbf))
            sb_ = hh % 2
            s0_h, s1_h, s2_h, s3_h = sig[sb_]
            s0B, s1B, s2B, s3B = B_sig[sb_]
            act(s0_h[:], pb[bg][:], AF.Relu, (pbB[bg],), (s0B,),
                scale=SF, bias=30.0)
            act(s1_h[:], s0_h[:], AF.Exp, (s0B,), (s1B,), scale=-1.0,
                bias=30.0)
            act(s0_h[:], s1_h[:], AF.Ln, (s1B,), (s0B,), bias=1.0)
            act(s1_h[:], s0_h[:], AF.Exp, (s0B,), (s1B,), scale=-1.0)
            act(s2_h[:], pb[bu][:], AF.Identity, (pbB[bu],), (s2B,), scale=SF)
            act(s3_h[:], pb[bg][:], AF.Identity, (pbB[bg],), (s3B,))
            tt(s0_h[:], s3_h[:], s2_h[:], ALU.mult, (s3B, s2B), (s0B,))
            stt(hbf[:, hh, :], s0_h[:], SF, s1_h[:], ALU.mult, ALU.mult,
                (s0B, s1B), (B_hbf,))
        if i + 1 < NT:
            ld(xp3[(i + 1) % 2][:], d_xp[:, :, (i + 1) * TT:(i + 2) * TT],
               (B_dxp[i + 1],), (B_xp3[(i + 1) % 2],))
            p3_front(i + 1)
        for j in range(DC):
            bank = 4 + j % 2
            for hc_ in range(HC):
                mm(bank, wd_sb[:, hc_, j * P:(j + 1) * P], hbf[:, hc_, :],
                   hc_ == 0, hc_ == HC - 1, (B_wd, B_hbf))
            act(y3[:, j, :], pb[bank][:], AF.Identity, (pbB[bank],), (B_y3,),
                scale=SF)
            act(xpbf[:, j, :], y3[:, j, :], AF.Identity, (B_y3,), (B_xpbf,))
            act(xp2bf[:, j, :], y3[:, j, :], AF.Square, (B_y3,), (B_xp2bf,))
        for c in range(DC):
            mm(6, ones_bf[:], xpbf[:, c, :], c == 0, c == DC - 1,
               (B_ones_bf, B_xpbf))
        for c in range(DC):
            mm(7, ones_bf[:], xp2bf[:, c, :], c == 0, c == DC - 1,
               (B_ones_bf, B_xp2bf))
        ln_stats(6, 7, m4, B_m4, sA4, B_sA4, sB4, B_sB4)
        tt(tmp3[:], y3[:], bc(m4, DC), ALU.subtract, (B_y3, B_m4), (B_tmp3,))
        tt(y3[:], tmp3[:], bc(sB4, DC), ALU.mult, (B_tmp3, B_sB4), (B_y3,))
        tt(tmp3[:], y3[:], xp3[xb][:], ALU.add, (B_y3, B_xp3[xb]), (B_tmp3,))
        st(dout[:, :, t0:t0 + TT], tmp3[:], (B_tmp3,), inc=True)

    ld(xp3[0][:], d_xp[:, :, 0:TT], (B_dxp[0],), (B_xp3[0],))
    p3_front(0)
    for i in range(NT):
        p3_tile(i)

    if debug:
        dbg_h = nc.dram_tensor("dbg_h", [P, HC, TT], F32,
                               kind="ExternalOutput").ap()
        dbg_y = nc.dram_tensor("dbg_y", [P, DC, TT], F32,
                               kind="ExternalOutput").ap()
        dbg_xh = nc.dram_tensor("dbg_xh", [P, DC, TT], F32,
                                kind="ExternalOutput").ap()
        st(dbg_h, hbf[:], (B_hbf,), inc=True)
        st(dbg_y, y3[:], (B_y3,), inc=True)
        st(dbg_xh, xhbf[:], (B_xhbf,), inc=True)

    p3.close()

    totals = sch.finalize()
    with nc.Block() as block:
        @block.tensor
        def _(e):
            sch.emit_stream("tensor", e, sems)

        @block.vector
        def _(e):
            sch.emit_stream("vector", e, sems)

        @block.scalar
        def _(e):
            sch.emit_stream("scalar", e, sems)

        @block.sync
        def _(e):
            sch.emit_stream("syq", e, sems)
            for lane in range(NLANES):
                t = totals.get(f"syq{lane}", 0)
                if t:
                    e.wait_ge(sems[f"syq{lane}"], t)

        @block.gpsimd
        def _(e):
            sch.emit_stream("gpq", e, sems)
            for lane in range(NLANES):
                t = totals.get(f"gpq{lane}", 0)
                if t:
                    e.wait_ge(sems[f"gpq{lane}"], t)

    es_all.close()
    pers.close()
    return nc


_NC_CACHE = None


def _get_nc():
    global _NC_CACHE
    if _NC_CACHE is None:
        _NC_CACHE = build_nc()
    return _NC_CACHE


def _fm_vec(v):
    return np.ascontiguousarray(np.asarray(v, np.float32).reshape(DC, P).T)


def _fm_mat(w):
    w = np.asarray(w)
    return np.ascontiguousarray(
        w.reshape(w.shape[0] // P, P, w.shape[1]).transpose(1, 0, 2))


def kernel(L0_attn_x, L0_attn_time_decay, L0_attn_time_first,
           L0_attn_time_mix_k, L0_attn_time_mix_v, L0_attn_time_mix_r,
           L0_attn_W_key, L0_attn_W_value, L0_attn_W_receptance,
           L0_attn_W_output, L0_ffn_w_gate, L0_ffn_w_up, L0_ffn_w_down):
    x = np.asarray(L0_attn_x, np.float32)
    Bsz = x.shape[0]
    base = {
        "wkt": _fm_mat(np.ascontiguousarray(np.asarray(L0_attn_W_key, np.float32).T)),
        "wvt": _fm_mat(np.ascontiguousarray(np.asarray(L0_attn_W_value, np.float32).T)),
        "wrt": _fm_mat(np.ascontiguousarray(np.asarray(L0_attn_W_receptance, np.float32).T)),
        "wot": _fm_mat(np.ascontiguousarray(np.asarray(L0_attn_W_output, np.float32).T)),
        "wgt": _fm_mat(np.ascontiguousarray(np.asarray(L0_ffn_w_gate, np.int32).T)),
        "wut": _fm_mat(np.ascontiguousarray(np.asarray(L0_ffn_w_up, np.int32).T)),
        "wdt": _fm_mat(np.ascontiguousarray(np.asarray(L0_ffn_w_down, np.int32).T)),
        "td": _fm_vec(L0_attn_time_decay),
        "tf": _fm_vec(L0_attn_time_first),
        "mk": _fm_vec(np.asarray(L0_attn_time_mix_k).reshape(-1)),
        "mv": _fm_vec(np.asarray(L0_attn_time_mix_v).reshape(-1)),
        "mr": _fm_vec(np.asarray(L0_attn_time_mix_r).reshape(-1)),
    }
    in_maps = []
    for b in range(Bsz):
        m = dict(base)
        m["xt"] = np.ascontiguousarray(
            x[b].T.reshape(DC, P, TSEQ).transpose(1, 0, 2))
        in_maps.append(m)

    nc = _get_nc()
    res = run_bass_kernel_spmd(nc, in_maps, core_ids=list(range(Bsz)))
    out = np.empty((Bsz, TSEQ, D), np.float32)
    for b in range(Bsz):
        o = np.asarray(res.results[b]["out"])
        out[b] = o.transpose(1, 0, 2).reshape(D, TSEQ).T
    return out
